# version 02
# baseline (speedup 1.0000x reference)
import numpy as np
import ml_dtypes
import concourse.bass as bass
import concourse.mybir as mybir
from concourse.bass_utils import run_bass_kernel_spmd

F32 = mybir.dt.float32
BF16 = mybir.dt.bfloat16
I8 = mybir.dt.int8
OP = mybir.AluOpType
AF = mybir.ActivationFunctionType
AX = mybir.AxisListType

D = 1024
DC = 8
NS = 64
TTILE = 256
G = 4

PI_LO = float(np.float32(3.1415925))
C1 = float(np.float32(2 * np.pi))
C2 = float(np.float64(2 * np.pi) - np.float64(np.float32(2 * np.pi)))
PIO2 = float(np.float32(np.pi / 2))
LN_EPS = 1e-5

ENGINES = ("sync", "tensor", "scalar", "vector", "gpsimd")


class Sem:
    __slots__ = ("h", "n")

    def __init__(self, h):
        self.h = h
        self.n = 0


class Env:
    pass


def inc(env, ins, sem, k=1):
    sem.n += k
    if ins is not None:
        ins.then_inc(sem.h, k)
    return sem.n


class DrainingVec:

    def __init__(self, e):
        self._e = e

    def __getattr__(self, name):
        f = getattr(self._e, name)

        def g(*a, **k):
            ins = f(*a, **k)
            self._e.drain()
            return ins
        return g


def mkW(env, e):
    cur = env.cur

    def W(sem, val):
        if val <= 0:
            return
        key = (cur, id(sem))
        if env.lastwait.get(key, 0) >= val:
            return
        env.lastwait[key] = val
        e.wait_ge(sem.h, val)

    return W


def fence(env, e, W):
    for s in (env.t, env.s, env.v, env.din, env.wst, env.sto):
        W(s, s.n)


def run_phase(block, env, fn):
    snap = [(s, s.n) for s in env.all_sems]
    lw = dict(env.lastwait)
    for name in ENGINES:
        for s, n in snap:
            s.n = n
        env.lastwait = dict(lw)
        env.cur = name

        def body(e, fn=fn):
            fn(env, e)

        getattr(block, name)(body)
        lw = env.lastwait



def phase_prologue(env, e):
    W = mkW(env, e)
    cur = env.cur
    c = env.c
    v_ = cur == "vector"
    s_ = cur == "scalar"
    sy = cur == "sync"
    din = env.din

    small = [
        (env.lAr_t[:], env.d_lAr[:]),
        (env.lAi_t[:], env.d_lAi[:]),
        (env.b1t[:], env.d_b1[:]),
        (env.phT[:], env.d_ph[:]),
        (env.qwrT[:], env.d_qwr[:]),
        (env.qwiT[:], env.d_qwi[:]),
        (env.b2row[:], env.d_b2[:]),
        (env.identf[:], env.d_idf[:]),
        (env.identb[:], env.d_idb[:]),
        (env.bwt_f.rearrange("p (dc n) -> p dc n", n=2 * NS),
         env.d_bwt.rearrange("(dc p) n -> p dc n", p=128)),
        (env.cwt_f[:], env.d_cwt[:]),
    ]
    for dst, src in small:
        ins = e.dma_start(dst, src) if sy else None
        inc(env, ins, din, 16)
    din_small = din.n
    ins = None
    if sy:
        ins = e.dma_start(
            env.x.rearrange("p (c d) -> p c d", d=D),
            env.d_u.rearrange("(c p) d -> p c d", p=128))
    env.din_u = inc(env, ins, din, 16)

    ins = None
    if s_:
        W(din, din_small)
        ins = e.activation(env.eA[:], env.lAr_t[:], AF.Exp)
    sExp = inc(env, ins, env.s)

    ins = None
    if v_:
        W(env.s, sExp)
        W(din, din_small)
        e = DrainingVec(e)
        t = env.tmp64
        e.tensor_scalar(t[0][:], env.eA[:], -0.5, None, OP.mult)
        e.tensor_scalar(t[1][:], env.lAi_t[:], 0.5, None, OP.mult)
        e.tensor_scalar(t[2][:], t[0][:], -1.0, 1.0, OP.mult, OP.add)
        e.tensor_tensor(t[3][:], t[2][:], t[2][:], OP.mult)
        e.tensor_tensor(t[4][:], t[1][:], t[1][:], OP.mult)
        e.tensor_tensor(t[5][:], t[3][:], t[4][:], OP.add)
        e.tensor_tensor(t[6][:], t[0][:], t[0][:], OP.mult)
        e.tensor_tensor(t[7][:], t[6][:], t[4][:], OP.add)
        e.tensor_scalar(t[6][:], t[7][:], -1.0, 1.0, OP.mult, OP.add)
        e.reciprocal(t[3][:], t[5][:])
        e.tensor_tensor(env.Are[:], t[6][:], t[3][:], OP.mult)
        e.tensor_tensor(t[7][:], t[1][:], t[3][:], OP.mult)
        e.tensor_scalar(env.Aim[:], t[7][:], 2.0, None, OP.mult)
        e.tensor_copy(env.tabAr[0:64, 0:1], env.Are[:])
        e.tensor_copy(env.tabAr[64:128, 0:1], env.Are[:])
        e.tensor_scalar(env.tabAi[0:64, 0:1], env.Aim[:], -1.0, None, OP.mult)
        e.tensor_copy(env.tabAi[64:128, 0:1], env.Aim[:])
        for j in range(c.NSCAN - 1):
            a = env.tabAr[:, j:j + 1]
            b = env.tabAi[:, j:j + 1]
            e.tensor_tensor(env.t128a[:], a, a, OP.mult)
            e.tensor_tensor(env.t128b[:], b, b, OP.mult)
            e.tensor_tensor(env.tabAr[:, j + 1:j + 2], env.t128a[:], env.t128b[:],
                            OP.subtract)
            e.tensor_tensor(env.t128a[:], a, b, OP.mult)
            e.tensor_scalar(env.tabAi[:, j + 1:j + 2], env.t128a[:], 2.0, None, OP.mult)
        e.memset(env.konst[:, 0:1], 0.1)
        e.memset(env.konst[:, 1:2], -0.1)
        e.tensor_copy(env.thA[:], env.phT[:])
        for _ in range(7):
            e.tensor_scalar(env.mg[:], env.thA[:], PI_LO, None, OP.is_gt)
            e.scalar_tensor_tensor(env.thB[:], env.mg[:], -C1, env.thA[:], OP.mult, OP.add)
            e.scalar_tensor_tensor(env.thA[:], env.mg[:], -C2, env.thB[:], OP.mult, OP.add)
            e.tensor_scalar(env.mg[:], env.thA[:], -PI_LO, None, OP.is_lt)
            e.scalar_tensor_tensor(env.thB[:], env.mg[:], C1, env.thA[:], OP.mult, OP.add)
            e.scalar_tensor_tensor(env.thA[:], env.mg[:], C2, env.thB[:], OP.mult, OP.add)
        e.tensor_scalar(env.thB[:], env.thA[:], PIO2, None, OP.add)
        e.tensor_scalar(env.mg[:], env.thB[:], PI_LO, None, OP.is_gt)
        e.scalar_tensor_tensor(env.thC[:], env.mg[:], -C1, env.thB[:], OP.mult, OP.add)
        ins = e.scalar_tensor_tensor(env.thB[:], env.mg[:], -C2, env.thC[:],
                                     OP.mult, OP.add)
    vWrap = inc(env, ins, env.v)

    ins = None
    if s_:
        W(env.v, vWrap)
        e.activation(env.sth[:], env.thA[:], AF.Sin)
        ins = e.activation(env.cth[:], env.thB[:], AF.Sin)
    sSin = inc(env, ins, env.s)

    ins = None
    if v_:
        W(env.s, sSin)
        e = DrainingVec(e)
        e.tensor_tensor(env.thA[:], env.cth[:], env.qwrT[:], OP.mult)
        e.tensor_tensor(env.thB[:], env.sth[:], env.qwiT[:], OP.mult)
        e.tensor_tensor(env.thC[:], env.thA[:], env.thB[:], OP.add)
        e.tensor_tensor(env.thA[:], env.thC[:], env.qwiT[:], OP.mult)
        e.tensor_scalar(env.thB[:], env.thA[:], 0.1, None, OP.mult)
        e.tensor_tensor(env.qa[:], env.thB[:], env.sth[:], OP.mult)
        e.tensor_tensor(env.qb[:], env.thB[:], env.cth[:], OP.mult)
        e.tensor_scalar(env.b1s[:], env.b1t[:], 0.1, None, OP.mult)
        e.tensor_scalar(env.pi2m[:], env.b1t[:], -0.1, PIO2, OP.mult, OP.add)
        e.tensor_copy(env.bwt_h[:], env.bwt_f[:])
        e.tensor_tensor(env.bwt_l[:], env.bwt_f[:], env.bwt_h[:], OP.subtract)
        e.tensor_copy(env.cwt_h[:], env.cwt_f[:])
        e.tensor_tensor(env.cwt_l[:], env.cwt_f[:], env.cwt_h[:], OP.subtract)
        e.tensor_scalar(env.b2s[:], env.b2row[:], 0.5, None, OP.mult)
        e.tensor_copy(env.b2h[:], env.b2s[:])
        e.tensor_tensor(env.b2l[:], env.b2s[:], env.b2h[:], OP.subtract)
        ins = e.memset(env.ones2[:], 1.0)
    vPre = inc(env, ins, env.v)
    for dst, src in ((env.b2hl[0:1, :], env.b2h[:]), (env.b2hl[1:2, :], env.b2l[:])):
        ins = None
        if sy:
            W(env.v, vPre)
            ins = e.dma_start(dst, src)
        inc(env, ins, din, 16)
    env.din_b2 = din.n


def phase_ln(env, e):
    W = mkW(env, e)
    cur = env.cur
    c = env.c
    v_ = cur == "vector"
    s_ = cur == "scalar"
    t_ = cur == "tensor"
    fence(env, e, W)
    TCH, T = c.TCH, c.T
    ins_v = ins_s = None
    for tc in range(TCH):
        xs = env.x[:, tc * D:(tc + 1) * D]
        if v_:
            W(env.din, env.din_u)
            ins_v = e.reduce_sum(env.s1[:, tc:tc + 1], xs, axis=AX.X)
        if s_:
            W(env.din, env.din_u)
            ins_s = e.activation(env.y[0][:], xs, AF.Square,
                                 accum_out=env.s2[:, tc:tc + 1])
    vS1 = inc(env, ins_v, env.v)
    sS2 = inc(env, ins_s, env.s)
    ins = None
    if v_:
        W(env.s, sS2)
        e.drain()
        ed = DrainingVec(e)
        ed.tensor_scalar(env.mean[:], env.s1[:], 1.0 / D, None, OP.mult)
        ed.tensor_scalar(env.ex2e[:], env.s2[:], 1.0 / D, LN_EPS, OP.mult, OP.add)
        ed.tensor_tensor(env.msq[:], env.mean[:], env.mean[:], OP.mult)
        ins = ed.scalar_tensor_tensor(env.ve[:], env.msq[:], -1.0, env.ex2e[:],
                                      OP.mult, OP.add)
    vVe = inc(env, ins, env.v)
    ins = None
    if s_:
        W(env.v, vVe)
        ins = e.activation(env.sv[:], env.ve[:], AF.Sqrt)
    sSq = inc(env, ins, env.s)
    ins = None
    if v_:
        W(env.s, sSq)
        ed = DrainingVec(e)
        ed.reciprocal(env.r0[:], env.sv[:])
        for _ in range(2):
            ed.tensor_tensor(env.na[:], env.r0[:], env.r0[:], OP.mult)
            ed.tensor_tensor(env.nb[:], env.ve[:], env.na[:], OP.mult)
            ed.tensor_scalar(env.na[:], env.nb[:], -0.5, 1.5, OP.mult, OP.add)
            ed.tensor_tensor(env.r0[:], env.r0[:], env.na[:], OP.mult)
        ins = ed.scalar_tensor_tensor(env.rB[:], env.mean[:], -1.0, env.r0[:],
                                      OP.mult, OP.mult)
    vSt = inc(env, ins, env.v)

    sY, tTr, vEv = {}, {}, {}
    Yhi3 = env.Yhi.rearrange("p (dc t) -> p dc t", t=T)
    Ylo3 = env.Ylo.rearrange("p (dc t) -> p dc t", t=T)
    for tc in range(TCH):
        par = tc % 2
        ins = None
        if s_:
            W(env.v, vSt)
            if tc >= 2:
                W(env.t, tTr[tc - 2])
            ins = e.activation(env.y[par][:], env.x[:, tc * D:(tc + 1) * D],
                               AF.Identity,
                               bias=env.rB[:, tc:tc + 1], scale=env.r0[:, tc:tc + 1])
        sY[tc] = inc(env, ins, env.s)
        ins = None
        if t_:
            W(env.s, sY[tc])
            if tc >= 2:
                W(env.v, vEv[tc - 2])
            for k in range(DC):
                b = 2 * par + k // 4
                ins = e.matmul(env.po[b][:, (k % 4) * 128:(k % 4) * 128 + 128],
                               env.y[par][:, k * 128:(k + 1) * 128], env.identf[:],
                               is_transpose=True, start=(k % 4 == 0), stop=(k % 4 == 3))
        tTr[tc] = inc(env, ins, env.t)
        ins = None
        if v_:
            W(env.t, tTr[tc])
            for b in range(2):
                src = env.po[2 * par + b].rearrange("p (k j) -> p k j", j=128)
                dhi = Yhi3[:, 4 * b:4 * b + 4, tc * 128:(tc + 1) * 128]
                dlo = Ylo3[:, 4 * b:4 * b + 4, tc * 128:(tc + 1) * 128]
                e.tensor_copy(dhi, src)
                ins = e.tensor_tensor(dlo, src, dhi, OP.subtract)
        vEv[tc] = inc(env, ins, env.v)


def phase_ffn(env, e, call):
    W = mkW(env, e)
    cur = env.cur
    c = env.c
    v_ = cur == "vector"
    s_ = cur == "scalar"
    t_ = cur == "tensor"
    sy = cur == "sync"
    fence(env, e, W)
    T, NTT, NG, FC = c.T, c.NTT, c.NG, c.FC

    pkg = env.pkg
    inslot = env.inslot
    wstIn, vSplit, stoPkg = {}, {}, {}

    def issue_in_dma(fc):
        ins = None
        if call == 0:
            if sy:
                W(env.v, vSplit.get(fc - 2, 0))
                ins = e.dma_start(inslot[fc % 2][:],
                                  env.d_wpack[fc * 128:(fc + 1) * 128, :])
        else:
            if sy:
                ins = e.dma_start(pkg[fc % G][:],
                                  env.d_wpk[fc * 128:(fc + 1) * 128, :])
        wstIn[fc] = inc(env, ins, env.wst, 16)

    def do_split(fc, tfree):
        k = fc % G
        src = inslot[fc % 2]
        ins = None
        if v_:
            W(env.wst, wstIn[fc])
            if tfree:
                W(env.t, tfree)
            W(env.sto, stoPkg.get(fc - G, 0))
            e.tensor_copy(pkg[k][:, 0:1024], src[:, 0:1024])
            e.tensor_scalar(pkg[k][:, 1024:2048], src[:, 1024:2048], 0.5, None,
                            OP.mult)
            e.tensor_tensor(pkg[k][:, 2048:3072], src[:, 0:1024],
                            pkg[k][:, 0:1024], OP.subtract)
            ins = e.scalar_tensor_tensor(pkg[k][:, 3072:4096], src[:, 1024:2048],
                                         0.5, pkg[k][:, 1024:2048],
                                         OP.mult, OP.subtract)
        vSplit[fc] = inc(env, ins, env.v)
        ins = None
        if s_:
            W(env.v, vSplit[fc])
            ins = e.dma_start(env.d_wpk[fc * 128:(fc + 1) * 128, :], pkg[k][:])
        stoPkg[fc] = inc(env, ins, env.sto, 16)
        if call == 0 and fc + 2 < FC:
            issue_in_dma(fc + 2)

    if call == 0:
        issue_in_dma(0)
        issue_in_dma(1)
        for k in range(min(G, FC)):
            do_split(k, None)
    else:
        for k in range(min(G, FC)):
            issue_in_dma(k)

    tW1, tW2, vV1, vV3, vV5 = {}, {}, {}, {}, {}
    vEvac = 0
    idx = 0
    for g in range(NG):
        for tt in range(NTT):
            for k in range(G):
                fc = g * G + k
                par = idx % 2
                w1h = pkg[k][:, 0:1024]
                w2h = pkg[k][:, 1024:2048]
                w1l = pkg[k][:, 2048:3072]
                w2l = pkg[k][:, 3072:4096]
                ph = env.ph[par][:, 0:TTILE]
                ins = None
                if t_:
                    W(env.v, vV1.get(idx - 2, 0))
                    if call == 0:
                        W(env.v, vSplit[fc])
                    else:
                        W(env.wst, wstIn[fc])
                    n = 0
                    for dc in range(DC):
                        for (wi, yi) in ((w1h, env.Yhi), (w1h, env.Ylo), (w1l, env.Yhi)):
                            rhs = yi[:, dc * T + tt * TTILE: dc * T + (tt + 1) * TTILE]
                            ins = e.matmul(ph, wi[:, dc * 128:(dc + 1) * 128], rhs,
                                           start=(n == 0), stop=(n == 23))
                            n += 1
                tW1[idx] = inc(env, ins, env.t)
                ins = None
                if s_:
                    W(env.t, tW1[idx])
                    W(env.v, vV3.get(idx - 2, 0))
                    e.activation(env.sinp[par][:], ph, AF.Sin,
                                 bias=env.b1s[:, fc:fc + 1], scale=env.konst[:, 0:1])
                    ins = e.activation(env.cosp[par][:], ph, AF.Sin,
                                       bias=env.pi2m[:, fc:fc + 1],
                                       scale=env.konst[:, 1:2])
                sSC = inc(env, ins, env.s)
                ins = None
                if v_:
                    W(env.s, sSC)
                    ins = e.scalar_tensor_tensor(env.actf[par][:], env.cosp[par][:],
                                                 env.qa[:, fc:fc + 1], ph,
                                                 OP.mult, OP.add)
                vV1[idx] = inc(env, ins, env.v)
                ins = None
                if v_:
                    e.scalar_tensor_tensor(env.cosp[par][:], env.sinp[par][:],
                                           env.qb[:, fc:fc + 1], env.actf[par][:],
                                           OP.mult, OP.add)
                    ins = e.tensor_scalar(env.actf[par][:], env.cosp[par][:],
                                          env.b1t[:, fc:fc + 1], 0.0, OP.add, OP.max)
                vV3[idx] = inc(env, ins, env.v)
                ins = None
                if v_:
                    W(env.t, tW2.get(idx - 2, 0))
                    e.tensor_copy(env.acth[par][:], env.actf[par][:])
                    ins = e.tensor_tensor(env.actl[par][:], env.actf[par][:],
                                          env.acth[par][:], OP.subtract)
                vV5[idx] = inc(env, ins, env.v)
                ins = None
                if t_:
                    W(env.v, vV5[idx])
                    if k == 0:
                        W(env.v, vEvac)
                    if g == 0 and k == 0:
                        for ts in range(2):
                            for dh in range(2):
                                e.matmul(env.po[ts * 2 + dh][:], env.ones2[:],
                                         env.b2hl[:, dh * 512:(dh + 1) * 512],
                                         start=True, stop=False)
                    for ts in range(2):
                        for li, (lh, whl) in enumerate(((env.acth[par], (w2h, w2l)),
                                                        (env.actl[par], (w2h,)))):
                            lhsT = lh[:, ts * 128:(ts + 1) * 128]
                            for wslab in whl:
                                first = (k == 0 and g != 0 and li == 0 and wslab is w2h)
                                last = (k == G - 1 and li == 1)
                                for dh in range(2):
                                    ins = e.matmul(env.po[ts * 2 + dh][:], lhsT,
                                                   wslab[:, dh * 512:(dh + 1) * 512],
                                                   start=first, stop=last)
                tW2[idx] = inc(env, ins, env.t)
                if tt == NTT - 1:
                    nfc = (g + 1) * G + k
                    if nfc < FC:
                        if call == 0:
                            do_split(nfc, tW2[idx])
                        else:
                            ins = None
                            if sy:
                                W(env.t, tW2[idx])
                                ins = e.dma_start(pkg[k][:],
                                                  env.d_wpk[nfc * 128:(nfc + 1) * 128, :])
                            wstIn[nfc] = inc(env, ins, env.wst, 16)
                idx += 1
            ins = None
            if v_:
                W(env.t, tW2[idx - 1])
                for ts in range(2):
                    for dh in range(2):
                        o = (tt * 2 + ts) * D + dh * 512
                        xsl = env.x[:, o:o + 512]
                        ins = e.tensor_tensor(xsl, env.po[ts * 2 + dh][:], xsl, OP.add)
            vEvac = inc(env, ins, env.v)


def phase_ssm(env, e):
    W = mkW(env, e)
    cur = env.cur
    c = env.c
    v_ = cur == "vector"
    t_ = cur == "tensor"
    sy = cur == "sync"
    fence(env, e, W)
    T, TCH = c.T, c.TCH
    S0, S1, T2, Ssw = env.S0, env.S1, env.T2, env.Ssw

    tBu, vBuEv = {}, {}
    NQ = T // 512
    for q in range(NQ):
        ins = None
        if t_:
            W(env.v, vBuEv.get(q - 2, 0))
            n = 0
            for dc in range(DC):
                for (wi, yi) in ((env.bwt_h, env.Yhi), (env.bwt_h, env.Ylo),
                                 (env.bwt_l, env.Yhi)):
                    rhs = yi[:, dc * T + q * 512: dc * T + q * 512 + 512]
                    ins = e.matmul(env.ph[q % 2][:], wi[:, dc * 128:(dc + 1) * 128],
                                   rhs, start=(n == 0), stop=(n == 23))
                    n += 1
        tBu[q] = inc(env, ins, env.t)
        ins = None
        if v_:
            W(env.t, tBu[q])
            ins = e.tensor_copy(S0[:, q * 512:q * 512 + 512], env.ph[q % 2][:])
        vBuEv[q] = inc(env, ins, env.v)

    wstD = {}
    for j in (0, 1):
        ins = None
        if sy:
            ins = e.dma_start(env.sD[j % 2][:], env.d_dwt[j * 128:(j + 1) * 128, :])
        wstD[j] = inc(env, ins, env.wst, 16)

    ins = None
    if v_:
        W(env.v, vBuEv[NQ - 1])
        e.tensor_copy(Ssw[0:64, :], S0[64:128, :])
        e.tensor_copy(Ssw[64:128, :], S0[0:64, :])
        bufs = [S0, S1]
        for ji in range(c.NSCAN):
            d = 1 << ji
            cb = bufs[ji % 2]
            nb = bufs[(ji + 1) % 2]
            e.tensor_copy(nb[:, 0:d], cb[:, 0:d])
            e.scalar_tensor_tensor(T2[:, d:T], cb[:, 0:T - d],
                                   env.tabAr[:, ji:ji + 1], cb[:, d:T],
                                   OP.mult, OP.add)
            e.scalar_tensor_tensor(nb[:, d:T], Ssw[:, 0:T - d],
                                   env.tabAi[:, ji:ji + 1], T2[:, d:T],
                                   OP.mult, OP.add)
            if ji < c.NSCAN - 1:
                e.tensor_copy(Ssw[0:64, :], nb[64:128, :])
                e.tensor_copy(Ssw[64:128, :], nb[0:64, :])
        H = bufs[c.NSCAN % 2]
        e.tensor_copy(env.Hhi[:], H[:])
        ins = e.tensor_tensor(env.Hlo[:], H[:], env.Hhi[:], OP.subtract)
    vH = inc(env, ins, env.v)

    vDsp = {}
    for j in range(DC):
        ins = None
        if v_:
            W(env.wst, wstD[j])
            if j == 0:
                W(env.v, vH)
            e.tensor_copy(env.dwth[:, j * 1024:(j + 1) * 1024], env.sD[j % 2][:])
            ins = e.tensor_tensor(env.dwtl[:, j * 1024:(j + 1) * 1024],
                                  env.sD[j % 2][:],
                                  env.dwth[:, j * 1024:(j + 1) * 1024], OP.subtract)
        vDsp[j] = inc(env, ins, env.v)
        if j + 2 < DC:
            ins = None
            if sy:
                W(env.v, vDsp[j])
                ins = e.dma_start(env.sD[(j + 2) % 2][:],
                                  env.d_dwt[(j + 2) * 128:(j + 3) * 128, :])
            wstD[j + 2] = inc(env, ins, env.wst, 16)

    tCD, vCD = {}, {}
    nmm = 3 + 3 * DC
    for dh in range(2):
        for tc in range(TCH):
            i = dh * TCH + tc
            ins = None
            if t_:
                W(env.v, vDsp[DC - 1])
                W(env.v, vCD.get(i - 4, 0))
                n = 0
                for (lh, rh) in ((env.Hhi, env.cwt_h), (env.Hhi, env.cwt_l),
                                 (env.Hlo, env.cwt_h)):
                    ins = e.matmul(env.po[i % 4][:], lh[:, tc * 128:(tc + 1) * 128],
                                   rh[:, dh * 512:dh * 512 + 512],
                                   start=(n == 0), stop=(n == nmm - 1))
                    n += 1
                for dc in range(DC):
                    for (lh, rh) in ((env.Yhi, env.dwth), (env.Yhi, env.dwtl),
                                     (env.Ylo, env.dwth)):
                        lhsT = lh[:, dc * T + tc * 128: dc * T + (tc + 1) * 128]
                        rhs = rh[:, dc * 1024 + dh * 512: dc * 1024 + dh * 512 + 512]
                        ins = e.matmul(env.po[i % 4][:], lhsT, rhs,
                                       start=(n == 0), stop=(n == nmm - 1))
                        n += 1
            tCD[i] = inc(env, ins, env.t)
            ins = None
            if v_:
                W(env.t, tCD[i])
                xsl = env.x[:, tc * D + dh * 512: tc * D + dh * 512 + 512]
                ins = e.tensor_tensor(xsl, env.po[i % 4][:], xsl, OP.add)
            vCD[i] = inc(env, ins, env.v)


def mk_dump(name, get_ap):
    def phase_dump(env, e):
        W = mkW(env, e)
        fence(env, e, W)
        ins = None
        if env.cur == "sync":
            ins = e.dma_start(env.dbg[name], get_ap(env))
        val = inc(env, ins, env.outs, 16)
        if env.cur == "sync":
            e.wait_ge(env.outs.h, val)
    return phase_dump


def phase_out(env, e):
    W = mkW(env, e)
    fence(env, e, W)
    ins = None
    if env.cur == "sync":
        ins = e.dma_start(
            env.d_out.rearrange("(c p) d -> p c d", p=128),
            env.x.rearrange("p (c d) -> p c d", d=D))
    val = inc(env, ins, env.outs, 16)
    if env.cur == "sync":
        e.wait_ge(env.outs.h, val)



class Cfg:
    pass


def build(T=2048, DF=4096, dbg=False):
    c = Cfg()
    c.T = T
    c.DF = DF
    c.TCH = T // 128
    c.NTT = T // TTILE
    c.FC = DF // 128
    c.NG = c.FC // G
    c.NSCAN = int(round(np.log2(T)))
    assert 2 ** c.NSCAN == T and c.NG * G == c.FC

    nc = bass.Bass("TRN2", target_bir_lowering=False, detect_race_conditions=False)
    env = Env()
    env.c = c
    env.lastwait = {}

    env.d_u = nc.dram_tensor("u", [T, D], F32, kind="ExternalInput").ap()
    env.d_lAr = nc.dram_tensor("lAr", [NS, 1], F32, kind="ExternalInput").ap()
    env.d_lAi = nc.dram_tensor("lAi", [NS, 1], F32, kind="ExternalInput").ap()
    env.d_bwt = nc.dram_tensor("bwt", [D, 2 * NS], F32, kind="ExternalInput").ap()
    env.d_cwt = nc.dram_tensor("cwt", [2 * NS, D], F32, kind="ExternalInput").ap()
    env.d_dwt = nc.dram_tensor("dwt", [D, D], F32, kind="ExternalInput").ap()
    env.d_wpack = nc.dram_tensor("wpack", [c.FC * 128, 2048], F32,
                                 kind="ExternalInput").ap()
    env.d_b1 = nc.dram_tensor("b1", [128, c.FC], F32, kind="ExternalInput").ap()
    env.d_ph = nc.dram_tensor("ph", [128, c.FC], F32, kind="ExternalInput").ap()
    env.d_qwr = nc.dram_tensor("qwr", [128, c.FC], F32, kind="ExternalInput").ap()
    env.d_qwi = nc.dram_tensor("qwi", [128, c.FC], F32, kind="ExternalInput").ap()
    env.d_b2 = nc.dram_tensor("b2", [1, D], F32, kind="ExternalInput").ap()
    env.d_idf = nc.dram_tensor("idf", [128, 128], F32, kind="ExternalInput").ap()
    env.d_idb = nc.dram_tensor("idb", [128, 128], BF16, kind="ExternalInput").ap()
    env.d_out = nc.dram_tensor("out", [T, D], F32, kind="ExternalOutput").ap()
    env.d_wpk = nc.dram_tensor("wpk", [c.FC * 128, 4096], BF16, kind="Internal").ap()

    A = lambda n, s, d: nc.alloc_sbuf_tensor(n, s, d).ap()
    env.x = A("x", [128, c.TCH * D], F32)
    env.Yhi = A("Yhi", [128, DC * T], BF16)
    env.Ylo = A("Ylo", [128, DC * T], BF16)
    env.arena = A("arena", [128, 49152], I8)

    env.lAr_t = A("lAr_t", [NS, 1], F32)
    env.lAi_t = A("lAi_t", [NS, 1], F32)
    env.eA = A("eA", [NS, 1], F32)
    env.tmp64 = [A(f"t64_{i}", [NS, 1], F32) for i in range(8)]
    env.Are = A("Are", [NS, 1], F32)
    env.Aim = A("Aim", [NS, 1], F32)
    env.tabAr = A("tabAr", [128, c.NSCAN], F32)
    env.tabAi = A("tabAi", [128, c.NSCAN], F32)
    env.t128a = A("t128a", [128, 1], F32)
    env.t128b = A("t128b", [128, 1], F32)
    env.konst = A("konst", [128, 2], F32)

    env.b1t = A("b1t", [128, c.FC], F32)
    env.phT = A("phT", [128, c.FC], F32)
    env.qwrT = A("qwrT", [128, c.FC], F32)
    env.qwiT = A("qwiT", [128, c.FC], F32)
    env.thA = A("thA", [128, c.FC], F32)
    env.thB = A("thB", [128, c.FC], F32)
    env.thC = A("thC", [128, c.FC], F32)
    env.mg = A("mg", [128, c.FC], F32)
    env.sth = A("sth", [128, c.FC], F32)
    env.cth = A("cth", [128, c.FC], F32)
    env.qa = A("qa", [128, c.FC], F32)
    env.qb = A("qb", [128, c.FC], F32)
    env.b1s = A("b1s", [128, c.FC], F32)
    env.pi2m = A("pi2m", [128, c.FC], F32)

    env.bwt_h = A("bwt_h", [128, 2 * NS * DC], BF16)
    env.bwt_l = A("bwt_l", [128, 2 * NS * DC], BF16)
    env.cwt_h = A("cwt_h", [2 * NS, D], BF16)
    env.cwt_l = A("cwt_l", [2 * NS, D], BF16)

    env.b2hl = A("b2hl", [2, D], BF16)
    env.ones2 = A("ones2", [2, 128], BF16)
    env.identf = A("identf", [128, 128], F32)
    env.identb = A("identb", [128, 128], BF16)

    env.s1 = A("s1", [128, c.TCH], F32)
    env.s2 = A("s2", [128, c.TCH], F32)
    env.mean = A("mean", [128, c.TCH], F32)
    env.ex2e = A("ex2e", [128, c.TCH], F32)
    env.msq = A("msq", [128, c.TCH], F32)
    env.ve = A("ve", [128, c.TCH], F32)
    env.sv = A("sv", [128, c.TCH], F32)
    env.r0 = A("r0", [128, c.TCH], F32)
    env.na = A("na", [128, c.TCH], F32)
    env.nb = A("nb", [128, c.TCH], F32)
    env.rB = A("rB", [128, c.TCH], F32)

    env.sinp = [A("sinp0", [128, TTILE], F32), A("sinp1", [128, TTILE], F32)]
    env.cosp = [A("cosp0", [128, TTILE], F32), A("cosp1", [128, TTILE], F32)]
    env.actf = [A("actf0", [128, TTILE], F32), A("actf1", [128, TTILE], F32)]
    env.acth = [A("acth0", [128, TTILE], BF16), A("acth1", [128, TTILE], BF16)]
    env.actl = [A("actl0", [128, TTILE], BF16), A("actl1", [128, TTILE], BF16)]

    def aview(off_b, nbytes, dt):
        return env.arena[:, off_b:off_b + nbytes].bitcast(dt)
    env.pkg = [aview(8192 * k, 8192, BF16) for k in range(4)]
    env.bwt_f = aview(0, 4096, F32)
    env.cwt_f = aview(4096, 4096, F32)
    env.b2row = aview(8192, 4096, F32)[0:1, :]
    env.b2s = aview(12288, 4096, F32)[0:1, :]
    env.b2h = aview(16384, 2048, BF16)[0:1, :]
    env.b2l = aview(18432, 2048, BF16)[0:1, :]
    env.y = [aview(40960, 4096, F32), aview(45056, 4096, F32)]
    env.inslot = [aview(32768 + 8192 * j, 8192, F32) for j in range(2)]
    env.S0 = aview(0, 8192, F32)
    env.S1 = aview(8192, 8192, F32)
    env.T2 = aview(16384, 8192, F32)
    env.Ssw = aview(24576, 8192, F32)
    env.Hhi = aview(32768, 4096, BF16)
    env.Hlo = aview(36864, 4096, BF16)
    env.dwth = aview(0, 16384, BF16)
    env.dwtl = aview(16384, 16384, BF16)
    env.sD = [aview(40960, 4096, F32), aview(45056, 4096, F32)]

    P = lambda n, s, d: nc.alloc_psum_tensor(n, s, d).ap()
    env.ph = [P("ph0", [128, 512], F32), P("ph1", [128, 512], F32)]
    env.po = [P(f"po{i}", [128, 512], F32) for i in range(4)]
    env.pl = [P("pl0", [128, 512], F32), P("pl1", [128, 512], F32)]

    env.t = Sem(nc.alloc_semaphore("s_t"))
    env.s = Sem(nc.alloc_semaphore("s_s"))
    env.v = Sem(nc.alloc_semaphore("s_v"))
    env.din = Sem(nc.alloc_semaphore("s_din"))
    env.wst = Sem(nc.alloc_semaphore("s_wst"))
    env.sto = Sem(nc.alloc_semaphore("s_sto"))
    env.outs = Sem(nc.alloc_semaphore("s_outs"))
    env.all_sems = [env.t, env.s, env.v, env.din, env.wst, env.sto, env.outs]

    env.dbg = {}
    if dbg:
        for nm, shape, dt in [("dYhi", [128, DC * T], BF16), ("dYlo", [128, DC * T], BF16),
                              ("dx1", [128, c.TCH * D], F32), ("dx2", [128, c.TCH * D], F32),
                              ("dqa", [128, c.FC], F32), ("dqb", [128, c.FC], F32),
                              ("dtabAr", [128, c.NSCAN], F32), ("dtabAi", [128, c.NSCAN], F32),
                              ("dS0", [128, T], F32), ("dH", [128, T], F32)]:
            env.dbg[nm] = nc.dram_tensor(nm, shape, dt, kind="ExternalOutput").ap()
    with nc.Block() as block:
        run_phase(block, env, phase_prologue)
        run_phase(block, env, phase_ln)
        if dbg:
            run_phase(block, env, mk_dump("dYhi", lambda env: env.Yhi))
            run_phase(block, env, mk_dump("dYlo", lambda env: env.Ylo))
            run_phase(block, env, mk_dump("dqa", lambda env: env.qa))
            run_phase(block, env, mk_dump("dqb", lambda env: env.qb))
            run_phase(block, env, mk_dump("dtabAr", lambda env: env.tabAr))
            run_phase(block, env, mk_dump("dtabAi", lambda env: env.tabAi))
        run_phase(block, env, lambda env, e: phase_ffn(env, e, 0))
        if dbg:
            run_phase(block, env, mk_dump("dx1", lambda env: env.x))
        run_phase(block, env, phase_ln)
        run_phase(block, env, phase_ssm)
        if dbg:
            run_phase(block, env, mk_dump("dx2", lambda env: env.x))
        run_phase(block, env, phase_ln)
        run_phase(block, env, lambda env, e: phase_ffn(env, e, 1))
        run_phase(block, env, phase_out)
    return nc



def prep_inputs(T, DF, u_b, lAr, lAi, Bw, Cw, Dw, W1, b1, ph, qwr, qwi, W2, b2):
    FC = DF // 128
    f32 = np.float32
    cont = np.ascontiguousarray
    W1p = np.asarray(W1, f32).reshape(DC, 128, FC, 128).transpose(2, 1, 0, 3) \
        .reshape(FC, 128, DC * 128)
    W2p = np.asarray(W2, f32).reshape(FC, 128, D)
    wpack = cont(np.concatenate([W1p, W2p], axis=2).reshape(FC * 128, 2048))
    colmaj = lambda v: cont(np.asarray(v, f32).reshape(FC, 128).T)
    return {
        "u": cont(np.asarray(u_b, f32)),
        "lAr": cont(np.asarray(lAr, f32).reshape(NS, 1)),
        "lAi": cont(np.asarray(lAi, f32).reshape(NS, 1)),
        "bwt": cont(np.asarray(Bw, f32).T),
        "cwt": cont(np.asarray(Cw, f32).T),
        "dwt": cont(np.asarray(Dw, f32).T),
        "wpack": wpack,
        "b1": colmaj(b1),
        "ph": colmaj(ph),
        "qwr": colmaj(qwr),
        "qwi": colmaj(qwi),
        "b2": cont(np.asarray(b2, f32).reshape(1, D)),
        "idf": np.eye(128, dtype=f32),
        "idb": np.eye(128, dtype=ml_dtypes.bfloat16),
    }


def kernel(L0_attn_u, L0_attn_log_A_real, L0_attn_log_A_imag, L0_attn_B_w,
           L0_attn_C_w, L0_attn_D_w, L0_ffn_W1, L0_ffn_b1, L0_ffn_quantum_phases,
           L0_ffn_quantum_weights_real, L0_ffn_quantum_weights_imag, L0_ffn_W2,
           L0_ffn_b2):
    T, DF = 2048, 4096
    u = np.asarray(L0_attn_u, np.float32)
    B = u.shape[0]
    nc = build(T, DF)
    in_maps = [
        prep_inputs(T, DF, u[b], L0_attn_log_A_real, L0_attn_log_A_imag,
                    L0_attn_B_w, L0_attn_C_w, L0_attn_D_w, L0_ffn_W1, L0_ffn_b1,
                    L0_ffn_quantum_phases, L0_ffn_quantum_weights_real,
                    L0_ffn_quantum_weights_imag, L0_ffn_W2, L0_ffn_b2)
        for b in range(B)
    ]
    res = run_bass_kernel_spmd(nc, in_maps, core_ids=list(range(B)))
    return np.stack([np.asarray(r["out"], np.float32) for r in res.results], axis=0)


# version 03
# speedup vs baseline: 1.7070x; 1.7070x over previous
import numpy as np
import ml_dtypes
import concourse.bass as bass
import concourse.mybir as mybir
from concourse.bass_utils import run_bass_kernel_spmd

F32 = mybir.dt.float32
BF16 = mybir.dt.bfloat16
I8 = mybir.dt.int8
OP = mybir.AluOpType
AF = mybir.ActivationFunctionType
AX = mybir.AxisListType

D = 1024
DC = 8
NS = 64
TTILE = 256
G = 4

PI_LO = float(np.float32(3.1415925))
C1 = float(np.float32(2 * np.pi))
C2 = float(np.float64(2 * np.pi) - np.float64(np.float32(2 * np.pi)))
PIO2 = float(np.float32(np.pi / 2))
LN_EPS = 1e-5

ENGINES = ("sync", "tensor", "scalar", "vector", "gpsimd")


class Sem:
    __slots__ = ("h", "n")

    def __init__(self, h):
        self.h = h
        self.n = 0


class Env:
    pass


def inc(env, ins, sem, k=1):
    sem.n += k
    if ins is not None:
        ins.then_inc(sem.h, k)
    return sem.n


class DrainingVec:

    def __init__(self, e):
        self._e = e

    def __getattr__(self, name):
        f = getattr(self._e, name)

        def g(*a, **k):
            ins = f(*a, **k)
            self._e.drain()
            return ins
        return g


def mkW(env, e):
    cur = env.cur

    def W(sem, val):
        if val <= 0:
            return
        key = (cur, id(sem))
        if env.lastwait.get(key, 0) >= val:
            return
        env.lastwait[key] = val
        e.wait_ge(sem.h, val)

    return W


def fence(env, e, W):
    for s in (env.t, env.s, env.v, env.din, env.wst, env.sto):
        W(s, s.n)


def run_phase(block, env, fn):
    snap = [(s, s.n) for s in env.all_sems]
    lw = dict(env.lastwait)
    for name in ENGINES:
        for s, n in snap:
            s.n = n
        env.lastwait = dict(lw)
        env.cur = name

        def body(e, fn=fn):
            fn(env, e)

        getattr(block, name)(body)
        lw = env.lastwait



def phase_prologue(env, e):
    W = mkW(env, e)
    cur = env.cur
    c = env.c
    v_ = cur == "vector"
    s_ = cur == "scalar"
    sy = cur == "sync"
    din = env.din

    small = [
        (env.lAr_t[:], env.d_lAr[:]),
        (env.lAi_t[:], env.d_lAi[:]),
        (env.b1t[:], env.d_b1[:]),
        (env.phT[:], env.d_ph[:]),
        (env.qwrT[:], env.d_qwr[:]),
        (env.qwiT[:], env.d_qwi[:]),
        (env.b2row[:], env.d_b2[:]),
        (env.identf[:], env.d_idf[:]),
        (env.identb[:], env.d_idb[:]),
        (env.bwt_f.rearrange("p (dc n) -> p dc n", n=2 * NS),
         env.d_bwt.rearrange("(dc p) n -> p dc n", p=128)),
        (env.cwt_f[:], env.d_cwt[:]),
    ]
    for dst, src in small:
        ins = e.dma_start(dst, src) if sy else None
        inc(env, ins, din, 16)
    din_small = din.n
    ins = None
    if sy:
        ins = e.dma_start(
            env.x.rearrange("p (c d) -> p c d", d=D),
            env.d_u.rearrange("(c p) d -> p c d", p=128))
    env.din_u = inc(env, ins, din, 16)

    ins = None
    if s_:
        W(din, din_small)
        ins = e.activation(env.eA[:], env.lAr_t[:], AF.Exp)
    sExp = inc(env, ins, env.s)

    ins = None
    if v_:
        W(env.s, sExp)
        W(din, din_small)
        e = DrainingVec(e)
        t = env.tmp64
        e.tensor_scalar(t[0][:], env.eA[:], -0.5, None, OP.mult)
        e.tensor_scalar(t[1][:], env.lAi_t[:], 0.5, None, OP.mult)
        e.tensor_scalar(t[2][:], t[0][:], -1.0, 1.0, OP.mult, OP.add)
        e.tensor_tensor(t[3][:], t[2][:], t[2][:], OP.mult)
        e.tensor_tensor(t[4][:], t[1][:], t[1][:], OP.mult)
        e.tensor_tensor(t[5][:], t[3][:], t[4][:], OP.add)
        e.tensor_tensor(t[6][:], t[0][:], t[0][:], OP.mult)
        e.tensor_tensor(t[7][:], t[6][:], t[4][:], OP.add)
        e.tensor_scalar(t[6][:], t[7][:], -1.0, 1.0, OP.mult, OP.add)
        e.reciprocal(t[3][:], t[5][:])
        e.tensor_tensor(env.Are[:], t[6][:], t[3][:], OP.mult)
        e.tensor_tensor(t[7][:], t[1][:], t[3][:], OP.mult)
        e.tensor_scalar(env.Aim[:], t[7][:], 2.0, None, OP.mult)
        e.tensor_copy(env.tabAr[0:64, 0:1], env.Are[:])
        e.tensor_copy(env.tabAr[64:128, 0:1], env.Are[:])
        e.tensor_scalar(env.tabAi[0:64, 0:1], env.Aim[:], -1.0, None, OP.mult)
        e.tensor_copy(env.tabAi[64:128, 0:1], env.Aim[:])
        for j in range(c.NSCAN - 1):
            a = env.tabAr[:, j:j + 1]
            b = env.tabAi[:, j:j + 1]
            e.tensor_tensor(env.t128a[:], a, a, OP.mult)
            e.tensor_tensor(env.t128b[:], b, b, OP.mult)
            e.tensor_tensor(env.tabAr[:, j + 1:j + 2], env.t128a[:], env.t128b[:],
                            OP.subtract)
            e.tensor_tensor(env.t128a[:], a, b, OP.mult)
            e.tensor_scalar(env.tabAi[:, j + 1:j + 2], env.t128a[:], 2.0, None, OP.mult)
        e.memset(env.konst[:, 0:1], 0.1)
        e.memset(env.konst[:, 1:2], -0.1)
        e.tensor_copy(env.thA[:], env.phT[:])
        for _ in range(7):
            e.tensor_scalar(env.mg[:], env.thA[:], PI_LO, None, OP.is_gt)
            e.scalar_tensor_tensor(env.thB[:], env.mg[:], -C1, env.thA[:], OP.mult, OP.add)
            e.scalar_tensor_tensor(env.thA[:], env.mg[:], -C2, env.thB[:], OP.mult, OP.add)
            e.tensor_scalar(env.mg[:], env.thA[:], -PI_LO, None, OP.is_lt)
            e.scalar_tensor_tensor(env.thB[:], env.mg[:], C1, env.thA[:], OP.mult, OP.add)
            e.scalar_tensor_tensor(env.thA[:], env.mg[:], C2, env.thB[:], OP.mult, OP.add)
        e.tensor_scalar(env.thB[:], env.thA[:], PIO2, None, OP.add)
        e.tensor_scalar(env.mg[:], env.thB[:], PI_LO, None, OP.is_gt)
        e.scalar_tensor_tensor(env.thC[:], env.mg[:], -C1, env.thB[:], OP.mult, OP.add)
        ins = e.scalar_tensor_tensor(env.thB[:], env.mg[:], -C2, env.thC[:],
                                     OP.mult, OP.add)
    vWrap = inc(env, ins, env.v)

    ins = None
    if s_:
        W(env.v, vWrap)
        e.activation(env.sth[:], env.thA[:], AF.Sin)
        ins = e.activation(env.cth[:], env.thB[:], AF.Sin)
    sSin = inc(env, ins, env.s)

    ins = None
    if v_:
        W(env.s, sSin)
        e = DrainingVec(e)
        e.tensor_tensor(env.thA[:], env.cth[:], env.qwrT[:], OP.mult)
        e.tensor_tensor(env.thB[:], env.sth[:], env.qwiT[:], OP.mult)
        e.tensor_tensor(env.thC[:], env.thA[:], env.thB[:], OP.add)
        e.tensor_tensor(env.thA[:], env.thC[:], env.qwiT[:], OP.mult)
        e.tensor_scalar(env.thB[:], env.thA[:], 0.1, None, OP.mult)
        e.tensor_tensor(env.qa[:], env.thB[:], env.sth[:], OP.mult)
        e.tensor_tensor(env.qb[:], env.thB[:], env.cth[:], OP.mult)
        e.tensor_scalar(env.b1s[:], env.b1t[:], 0.1, None, OP.mult)
        e.tensor_scalar(env.pi2m[:], env.b1t[:], -0.1, PIO2, OP.mult, OP.add)
        e.tensor_copy(env.bwt_h[:], env.bwt_f[:])
        e.tensor_tensor(env.bwt_l[:], env.bwt_f[:], env.bwt_h[:], OP.subtract)
        e.tensor_copy(env.cwt_h[:], env.cwt_f[:])
        e.tensor_tensor(env.cwt_l[:], env.cwt_f[:], env.cwt_h[:], OP.subtract)
        e.tensor_scalar(env.b2s[:], env.b2row[:], 0.5, None, OP.mult)
        e.tensor_copy(env.b2h[:], env.b2s[:])
        e.tensor_tensor(env.b2l[:], env.b2s[:], env.b2h[:], OP.subtract)
        ins = e.memset(env.ones2[:], 1.0)
    vPre = inc(env, ins, env.v)
    for dst, src in ((env.b2hl[0:1, :], env.b2h[:]), (env.b2hl[1:2, :], env.b2l[:])):
        ins = None
        if sy:
            W(env.v, vPre)
            ins = e.dma_start(dst, src)
        inc(env, ins, din, 16)
    env.din_b2 = din.n


def phase_ln(env, e):
    W = mkW(env, e)
    cur = env.cur
    c = env.c
    v_ = cur == "vector"
    s_ = cur == "scalar"
    t_ = cur == "tensor"
    fence(env, e, W)
    TCH, T = c.TCH, c.T
    ins_v = ins_s = None
    for tc in range(TCH):
        xs = env.x[:, tc * D:(tc + 1) * D]
        if v_:
            W(env.din, env.din_u)
            ins_v = e.reduce_sum(env.s1[:, tc:tc + 1], xs, axis=AX.X)
        if s_:
            W(env.din, env.din_u)
            ins_s = e.activation(env.y[0][:], xs, AF.Square,
                                 accum_out=env.s2[:, tc:tc + 1])
    vS1 = inc(env, ins_v, env.v)
    sS2 = inc(env, ins_s, env.s)
    ins = None
    if v_:
        W(env.s, sS2)
        e.drain()
        ed = DrainingVec(e)
        ed.tensor_scalar(env.mean[:], env.s1[:], 1.0 / D, None, OP.mult)
        ed.tensor_scalar(env.ex2e[:], env.s2[:], 1.0 / D, LN_EPS, OP.mult, OP.add)
        ed.tensor_tensor(env.msq[:], env.mean[:], env.mean[:], OP.mult)
        ins = ed.scalar_tensor_tensor(env.ve[:], env.msq[:], -1.0, env.ex2e[:],
                                      OP.mult, OP.add)
    vVe = inc(env, ins, env.v)
    ins = None
    if s_:
        W(env.v, vVe)
        ins = e.activation(env.sv[:], env.ve[:], AF.Sqrt)
    sSq = inc(env, ins, env.s)
    ins = None
    if v_:
        W(env.s, sSq)
        ed = DrainingVec(e)
        ed.reciprocal(env.r0[:], env.sv[:])
        for _ in range(2):
            ed.tensor_tensor(env.na[:], env.r0[:], env.r0[:], OP.mult)
            ed.tensor_tensor(env.nb[:], env.ve[:], env.na[:], OP.mult)
            ed.tensor_scalar(env.na[:], env.nb[:], -0.5, 1.5, OP.mult, OP.add)
            ed.tensor_tensor(env.r0[:], env.r0[:], env.na[:], OP.mult)
        ins = ed.scalar_tensor_tensor(env.rB[:], env.mean[:], -1.0, env.r0[:],
                                      OP.mult, OP.mult)
    vSt = inc(env, ins, env.v)

    sY, tTr, vEv = {}, {}, {}
    Yhi3 = env.Yhi.rearrange("p (dc t) -> p dc t", t=T)
    Ylo3 = env.Ylo.rearrange("p (dc t) -> p dc t", t=T)
    for tc in range(TCH):
        par = tc % 2
        ins = None
        if s_:
            W(env.v, vSt)
            if tc >= 2:
                W(env.t, tTr[tc - 2])
            ins = e.activation(env.y[par][:], env.x[:, tc * D:(tc + 1) * D],
                               AF.Identity,
                               bias=env.rB[:, tc:tc + 1], scale=env.r0[:, tc:tc + 1])
        sY[tc] = inc(env, ins, env.s)
        ins = None
        if t_:
            W(env.s, sY[tc])
            if tc >= 2:
                W(env.v, vEv[tc - 2])
            for k in range(DC):
                b = 2 * par + k // 4
                ins = e.matmul(env.po[b][:, (k % 4) * 128:(k % 4) * 128 + 128],
                               env.y[par][:, k * 128:(k + 1) * 128], env.identf[:],
                               is_transpose=True, start=(k % 4 == 0), stop=(k % 4 == 3))
        tTr[tc] = inc(env, ins, env.t)
        ins = None
        if v_:
            W(env.t, tTr[tc])
            for b in range(2):
                src = env.po[2 * par + b].rearrange("p (k j) -> p k j", j=128)
                dhi = Yhi3[:, 4 * b:4 * b + 4, tc * 128:(tc + 1) * 128]
                dlo = Ylo3[:, 4 * b:4 * b + 4, tc * 128:(tc + 1) * 128]
                e.tensor_copy(dhi, src)
                ins = e.tensor_tensor(dlo, src, dhi, OP.subtract)
        vEv[tc] = inc(env, ins, env.v)


def phase_ffn(env, e, call):
    W = mkW(env, e)
    cur = env.cur
    c = env.c
    v_ = cur == "vector"
    s_ = cur == "scalar"
    t_ = cur == "tensor"
    sy = cur == "sync"
    fence(env, e, W)
    T, NTT, NG, FC = c.T, c.NTT, c.NG, c.FC

    pkg = env.pkg
    inslot = env.inslot
    wstIn, vSplit, stoPkg = {}, {}, {}

    def issue_in_dma(fc):
        ins = None
        if call == 0:
            if sy:
                W(env.v, vSplit.get(fc - 2, 0))
                ins = e.dma_start(inslot[fc % 2][:],
                                  env.d_wpack[fc * 128:(fc + 1) * 128, :])
        else:
            if sy:
                ins = e.dma_start(pkg[fc % G][:],
                                  env.d_wpk[fc * 128:(fc + 1) * 128, :])
        wstIn[fc] = inc(env, ins, env.wst, 16)

    def do_split(fc, tfree):
        k = fc % G
        src = inslot[fc % 2]
        ins = None
        if v_:
            W(env.wst, wstIn[fc])
            if tfree:
                W(env.t, tfree)
            W(env.sto, stoPkg.get(fc - G, 0))
            e.tensor_copy(pkg[k][:, 0:1024], src[:, 0:1024])
            e.tensor_scalar(pkg[k][:, 1024:2048], src[:, 1024:2048], 0.5, None,
                            OP.mult)
            e.tensor_tensor(pkg[k][:, 2048:3072], src[:, 0:1024],
                            pkg[k][:, 0:1024], OP.subtract)
            ins = e.scalar_tensor_tensor(pkg[k][:, 3072:4096], src[:, 1024:2048],
                                         0.5, pkg[k][:, 1024:2048],
                                         OP.mult, OP.subtract)
        vSplit[fc] = inc(env, ins, env.v)
        ins = None
        if s_:
            W(env.v, vSplit[fc])
            ins = e.dma_start(env.d_wpk[fc * 128:(fc + 1) * 128, :], pkg[k][:])
        stoPkg[fc] = inc(env, ins, env.sto, 16)
        if call == 0 and fc + 2 < FC:
            issue_in_dma(fc + 2)

    if call == 0:
        issue_in_dma(0)
        issue_in_dma(1)
        for k in range(min(G, FC)):
            do_split(k, None)
    else:
        for k in range(min(G, FC)):
            issue_in_dma(k)

    tW1, tW2, vV1, vV3, vV5 = {}, {}, {}, {}, {}
    vEvac = 0
    slots = [(g, tt, k) for g in range(NG) for tt in range(NTT) for k in range(G)]

    def emit_w2(i):
        nonlocal vEvac
        g, tt, k = slots[i]
        par = i % 2
        w2h = pkg[k][:, 1024:2048]
        w2l = pkg[k][:, 3072:4096]
        ins = None
        if t_:
            W(env.v, vV5[i])
            if k == 0:
                W(env.v, vEvac)
            if g == 0 and k == 0:
                for ts in range(2):
                    for dh in range(2):
                        e.matmul(env.po[ts * 2 + dh][:], env.ones2[:],
                                 env.b2hl[:, dh * 512:(dh + 1) * 512],
                                 start=True, stop=False)
            for ts in range(2):
                for li, (lh, whl) in enumerate(((env.acth[par], (w2h, w2l)),
                                                (env.actl[par], (w2h,)))):
                    lhsT = lh[:, ts * 128:(ts + 1) * 128]
                    for wslab in whl:
                        first = (k == 0 and g != 0 and li == 0 and wslab is w2h)
                        last = (k == G - 1 and li == 1)
                        for dh in range(2):
                            ins = e.matmul(env.po[ts * 2 + dh][:], lhsT,
                                           wslab[:, dh * 512:(dh + 1) * 512],
                                           start=first, stop=last)
        tW2[i] = inc(env, ins, env.t)
        if tt == NTT - 1:
            nfc = (g + 1) * G + k
            if nfc < FC:
                if call == 0:
                    do_split(nfc, tW2[i])
                else:
                    ins = None
                    if sy:
                        W(env.t, tW2[i])
                        ins = e.dma_start(pkg[k][:],
                                          env.d_wpk[nfc * 128:(nfc + 1) * 128, :])
                    wstIn[nfc] = inc(env, ins, env.wst, 16)
        if k == G - 1:
            ins = None
            if v_:
                W(env.t, tW2[i])
                for ts in range(2):
                    for dh in range(2):
                        o = (tt * 2 + ts) * D + dh * 512
                        xsl = env.x[:, o:o + 512]
                        ins = e.tensor_tensor(xsl, env.po[ts * 2 + dh][:], xsl, OP.add)
            vEvac = inc(env, ins, env.v)

    for idx, (g, tt, k) in enumerate(slots):
        fc = g * G + k
        par = idx % 2
        w1h = pkg[k][:, 0:1024]
        w1l = pkg[k][:, 2048:3072]
        ph = env.ph[par][:, 0:TTILE]
        ins = None
        if t_:
            W(env.v, vV1.get(idx - 2, 0))
            if call == 0:
                W(env.v, vSplit[fc])
            else:
                W(env.wst, wstIn[fc])
            n = 0
            for dc in range(DC):
                for (wi, yi) in ((w1h, env.Yhi), (w1h, env.Ylo), (w1l, env.Yhi)):
                    rhs = yi[:, dc * T + tt * TTILE: dc * T + (tt + 1) * TTILE]
                    ins = e.matmul(ph, wi[:, dc * 128:(dc + 1) * 128], rhs,
                                   start=(n == 0), stop=(n == 23))
                    n += 1
        tW1[idx] = inc(env, ins, env.t)
        ins = None
        if s_:
            W(env.t, tW1[idx])
            W(env.v, vV3.get(idx - 2, 0))
            e.activation(env.sinp[par][:], ph, AF.Sin,
                         bias=env.b1s[:, fc:fc + 1], scale=env.konst[:, 0:1])
            ins = e.activation(env.cosp[par][:], ph, AF.Sin,
                               bias=env.pi2m[:, fc:fc + 1],
                               scale=env.konst[:, 1:2])
        sSC = inc(env, ins, env.s)
        ins = None
        if v_:
            W(env.s, sSC)
            ins = e.scalar_tensor_tensor(env.actf[par][:], env.cosp[par][:],
                                         env.qa[:, fc:fc + 1], ph,
                                         OP.mult, OP.add)
        vV1[idx] = inc(env, ins, env.v)
        ins = None
        if v_:
            e.scalar_tensor_tensor(env.cosp[par][:], env.sinp[par][:],
                                   env.qb[:, fc:fc + 1], env.actf[par][:],
                                   OP.mult, OP.add)
            ins = e.tensor_scalar(env.actf[par][:], env.cosp[par][:],
                                  env.b1t[:, fc:fc + 1], 0.0, OP.add, OP.max)
        vV3[idx] = inc(env, ins, env.v)
        ins = None
        if v_:
            W(env.t, tW2.get(idx - 2, 0))
            e.tensor_copy(env.acth[par][:], env.actf[par][:])
            ins = e.tensor_tensor(env.actl[par][:], env.actf[par][:],
                                  env.acth[par][:], OP.subtract)
        vV5[idx] = inc(env, ins, env.v)
        if idx >= 1:
            emit_w2(idx - 1)
    emit_w2(len(slots) - 1)


def phase_ssm(env, e):
    W = mkW(env, e)
    cur = env.cur
    c = env.c
    v_ = cur == "vector"
    t_ = cur == "tensor"
    sy = cur == "sync"
    fence(env, e, W)
    T, TCH = c.T, c.TCH
    S0, S1, T2, Ssw = env.S0, env.S1, env.T2, env.Ssw

    tBu, vBuEv = {}, {}
    NQ = T // 512
    for q in range(NQ):
        ins = None
        if t_:
            W(env.v, vBuEv.get(q - 2, 0))
            n = 0
            for dc in range(DC):
                for (wi, yi) in ((env.bwt_h, env.Yhi), (env.bwt_h, env.Ylo),
                                 (env.bwt_l, env.Yhi)):
                    rhs = yi[:, dc * T + q * 512: dc * T + q * 512 + 512]
                    ins = e.matmul(env.ph[q % 2][:], wi[:, dc * 128:(dc + 1) * 128],
                                   rhs, start=(n == 0), stop=(n == 23))
                    n += 1
        tBu[q] = inc(env, ins, env.t)
        ins = None
        if v_:
            W(env.t, tBu[q])
            ins = e.tensor_copy(S0[:, q * 512:q * 512 + 512], env.ph[q % 2][:])
        vBuEv[q] = inc(env, ins, env.v)

    wstD = {}
    for j in (0, 1):
        ins = None
        if sy:
            ins = e.dma_start(env.sD[j % 2][:], env.d_dwt[j * 128:(j + 1) * 128, :])
        wstD[j] = inc(env, ins, env.wst, 16)

    ins = None
    if v_:
        W(env.v, vBuEv[NQ - 1])
        e.tensor_copy(Ssw[0:64, :], S0[64:128, :])
        e.tensor_copy(Ssw[64:128, :], S0[0:64, :])
        bufs = [S0, S1]
        for ji in range(c.NSCAN):
            d = 1 << ji
            cb = bufs[ji % 2]
            nb = bufs[(ji + 1) % 2]
            e.tensor_copy(nb[:, 0:d], cb[:, 0:d])
            e.scalar_tensor_tensor(T2[:, d:T], cb[:, 0:T - d],
                                   env.tabAr[:, ji:ji + 1], cb[:, d:T],
                                   OP.mult, OP.add)
            e.scalar_tensor_tensor(nb[:, d:T], Ssw[:, 0:T - d],
                                   env.tabAi[:, ji:ji + 1], T2[:, d:T],
                                   OP.mult, OP.add)
            if ji < c.NSCAN - 1:
                e.tensor_copy(Ssw[0:64, :], nb[64:128, :])
                e.tensor_copy(Ssw[64:128, :], nb[0:64, :])
        H = bufs[c.NSCAN % 2]
        e.tensor_copy(env.Hhi[:], H[:])
        ins = e.tensor_tensor(env.Hlo[:], H[:], env.Hhi[:], OP.subtract)
    vH = inc(env, ins, env.v)

    vDsp = {}
    for j in range(DC):
        ins = None
        if v_:
            W(env.wst, wstD[j])
            if j == 0:
                W(env.v, vH)
            e.tensor_copy(env.dwth[:, j * 1024:(j + 1) * 1024], env.sD[j % 2][:])
            ins = e.tensor_tensor(env.dwtl[:, j * 1024:(j + 1) * 1024],
                                  env.sD[j % 2][:],
                                  env.dwth[:, j * 1024:(j + 1) * 1024], OP.subtract)
        vDsp[j] = inc(env, ins, env.v)
        if j + 2 < DC:
            ins = None
            if sy:
                W(env.v, vDsp[j])
                ins = e.dma_start(env.sD[(j + 2) % 2][:],
                                  env.d_dwt[(j + 2) * 128:(j + 3) * 128, :])
            wstD[j + 2] = inc(env, ins, env.wst, 16)

    tCD, vCD = {}, {}
    nmm = 3 + 3 * DC
    for dh in range(2):
        for tc in range(TCH):
            i = dh * TCH + tc
            ins = None
            if t_:
                W(env.v, vDsp[DC - 1])
                W(env.v, vCD.get(i - 4, 0))
                n = 0
                for (lh, rh) in ((env.Hhi, env.cwt_h), (env.Hhi, env.cwt_l),
                                 (env.Hlo, env.cwt_h)):
                    ins = e.matmul(env.po[i % 4][:], lh[:, tc * 128:(tc + 1) * 128],
                                   rh[:, dh * 512:dh * 512 + 512],
                                   start=(n == 0), stop=(n == nmm - 1))
                    n += 1
                for dc in range(DC):
                    for (lh, rh) in ((env.Yhi, env.dwth), (env.Yhi, env.dwtl),
                                     (env.Ylo, env.dwth)):
                        lhsT = lh[:, dc * T + tc * 128: dc * T + (tc + 1) * 128]
                        rhs = rh[:, dc * 1024 + dh * 512: dc * 1024 + dh * 512 + 512]
                        ins = e.matmul(env.po[i % 4][:], lhsT, rhs,
                                       start=(n == 0), stop=(n == nmm - 1))
                        n += 1
            tCD[i] = inc(env, ins, env.t)
            ins = None
            if v_:
                W(env.t, tCD[i])
                xsl = env.x[:, tc * D + dh * 512: tc * D + dh * 512 + 512]
                ins = e.tensor_tensor(xsl, env.po[i % 4][:], xsl, OP.add)
            vCD[i] = inc(env, ins, env.v)


def mk_dump(name, get_ap):
    def phase_dump(env, e):
        W = mkW(env, e)
        fence(env, e, W)
        ins = None
        if env.cur == "sync":
            ins = e.dma_start(env.dbg[name], get_ap(env))
        val = inc(env, ins, env.outs, 16)
        if env.cur == "sync":
            e.wait_ge(env.outs.h, val)
    return phase_dump


def phase_out(env, e):
    W = mkW(env, e)
    fence(env, e, W)
    ins = None
    if env.cur == "sync":
        ins = e.dma_start(
            env.d_out.rearrange("(c p) d -> p c d", p=128),
            env.x.rearrange("p (c d) -> p c d", d=D))
    val = inc(env, ins, env.outs, 16)
    if env.cur == "sync":
        e.wait_ge(env.outs.h, val)



class Cfg:
    pass


def build(T=2048, DF=4096, dbg=False):
    c = Cfg()
    c.T = T
    c.DF = DF
    c.TCH = T // 128
    c.NTT = T // TTILE
    c.FC = DF // 128
    c.NG = c.FC // G
    c.NSCAN = int(round(np.log2(T)))
    assert 2 ** c.NSCAN == T and c.NG * G == c.FC

    nc = bass.Bass("TRN2", target_bir_lowering=False, detect_race_conditions=False)
    env = Env()
    env.c = c
    env.lastwait = {}

    env.d_u = nc.dram_tensor("u", [T, D], F32, kind="ExternalInput").ap()
    env.d_lAr = nc.dram_tensor("lAr", [NS, 1], F32, kind="ExternalInput").ap()
    env.d_lAi = nc.dram_tensor("lAi", [NS, 1], F32, kind="ExternalInput").ap()
    env.d_bwt = nc.dram_tensor("bwt", [D, 2 * NS], F32, kind="ExternalInput").ap()
    env.d_cwt = nc.dram_tensor("cwt", [2 * NS, D], F32, kind="ExternalInput").ap()
    env.d_dwt = nc.dram_tensor("dwt", [D, D], F32, kind="ExternalInput").ap()
    env.d_wpack = nc.dram_tensor("wpack", [c.FC * 128, 2048], F32,
                                 kind="ExternalInput").ap()
    env.d_b1 = nc.dram_tensor("b1", [128, c.FC], F32, kind="ExternalInput").ap()
    env.d_ph = nc.dram_tensor("ph", [128, c.FC], F32, kind="ExternalInput").ap()
    env.d_qwr = nc.dram_tensor("qwr", [128, c.FC], F32, kind="ExternalInput").ap()
    env.d_qwi = nc.dram_tensor("qwi", [128, c.FC], F32, kind="ExternalInput").ap()
    env.d_b2 = nc.dram_tensor("b2", [1, D], F32, kind="ExternalInput").ap()
    env.d_idf = nc.dram_tensor("idf", [128, 128], F32, kind="ExternalInput").ap()
    env.d_idb = nc.dram_tensor("idb", [128, 128], BF16, kind="ExternalInput").ap()
    env.d_out = nc.dram_tensor("out", [T, D], F32, kind="ExternalOutput").ap()
    env.d_wpk = nc.dram_tensor("wpk", [c.FC * 128, 4096], BF16, kind="Internal").ap()

    A = lambda n, s, d: nc.alloc_sbuf_tensor(n, s, d).ap()
    env.x = A("x", [128, c.TCH * D], F32)
    env.Yhi = A("Yhi", [128, DC * T], BF16)
    env.Ylo = A("Ylo", [128, DC * T], BF16)
    env.arena = A("arena", [128, 49152], I8)

    env.lAr_t = A("lAr_t", [NS, 1], F32)
    env.lAi_t = A("lAi_t", [NS, 1], F32)
    env.eA = A("eA", [NS, 1], F32)
    env.tmp64 = [A(f"t64_{i}", [NS, 1], F32) for i in range(8)]
    env.Are = A("Are", [NS, 1], F32)
    env.Aim = A("Aim", [NS, 1], F32)
    env.tabAr = A("tabAr", [128, c.NSCAN], F32)
    env.tabAi = A("tabAi", [128, c.NSCAN], F32)
    env.t128a = A("t128a", [128, 1], F32)
    env.t128b = A("t128b", [128, 1], F32)
    env.konst = A("konst", [128, 2], F32)

    env.b1t = A("b1t", [128, c.FC], F32)
    env.phT = A("phT", [128, c.FC], F32)
    env.qwrT = A("qwrT", [128, c.FC], F32)
    env.qwiT = A("qwiT", [128, c.FC], F32)
    env.thA = A("thA", [128, c.FC], F32)
    env.thB = A("thB", [128, c.FC], F32)
    env.thC = A("thC", [128, c.FC], F32)
    env.mg = A("mg", [128, c.FC], F32)
    env.sth = A("sth", [128, c.FC], F32)
    env.cth = A("cth", [128, c.FC], F32)
    env.qa = A("qa", [128, c.FC], F32)
    env.qb = A("qb", [128, c.FC], F32)
    env.b1s = A("b1s", [128, c.FC], F32)
    env.pi2m = A("pi2m", [128, c.FC], F32)

    env.bwt_h = A("bwt_h", [128, 2 * NS * DC], BF16)
    env.bwt_l = A("bwt_l", [128, 2 * NS * DC], BF16)
    env.cwt_h = A("cwt_h", [2 * NS, D], BF16)
    env.cwt_l = A("cwt_l", [2 * NS, D], BF16)

    env.b2hl = A("b2hl", [2, D], BF16)
    env.ones2 = A("ones2", [2, 128], BF16)
    env.identf = A("identf", [128, 128], F32)
    env.identb = A("identb", [128, 128], BF16)

    env.s1 = A("s1", [128, c.TCH], F32)
    env.s2 = A("s2", [128, c.TCH], F32)
    env.mean = A("mean", [128, c.TCH], F32)
    env.ex2e = A("ex2e", [128, c.TCH], F32)
    env.msq = A("msq", [128, c.TCH], F32)
    env.ve = A("ve", [128, c.TCH], F32)
    env.sv = A("sv", [128, c.TCH], F32)
    env.r0 = A("r0", [128, c.TCH], F32)
    env.na = A("na", [128, c.TCH], F32)
    env.nb = A("nb", [128, c.TCH], F32)
    env.rB = A("rB", [128, c.TCH], F32)

    env.sinp = [A("sinp0", [128, TTILE], F32), A("sinp1", [128, TTILE], F32)]
    env.cosp = [A("cosp0", [128, TTILE], F32), A("cosp1", [128, TTILE], F32)]
    env.actf = [A("actf0", [128, TTILE], F32), A("actf1", [128, TTILE], F32)]
    env.acth = [A("acth0", [128, TTILE], BF16), A("acth1", [128, TTILE], BF16)]
    env.actl = [A("actl0", [128, TTILE], BF16), A("actl1", [128, TTILE], BF16)]

    def aview(off_b, nbytes, dt):
        return env.arena[:, off_b:off_b + nbytes].bitcast(dt)
    env.pkg = [aview(8192 * k, 8192, BF16) for k in range(4)]
    env.bwt_f = aview(0, 4096, F32)
    env.cwt_f = aview(4096, 4096, F32)
    env.b2row = aview(8192, 4096, F32)[0:1, :]
    env.b2s = aview(12288, 4096, F32)[0:1, :]
    env.b2h = aview(16384, 2048, BF16)[0:1, :]
    env.b2l = aview(18432, 2048, BF16)[0:1, :]
    env.y = [aview(40960, 4096, F32), aview(45056, 4096, F32)]
    env.inslot = [aview(32768 + 8192 * j, 8192, F32) for j in range(2)]
    env.S0 = aview(0, 8192, F32)
    env.S1 = aview(8192, 8192, F32)
    env.T2 = aview(16384, 8192, F32)
    env.Ssw = aview(24576, 8192, F32)
    env.Hhi = aview(32768, 4096, BF16)
    env.Hlo = aview(36864, 4096, BF16)
    env.dwth = aview(0, 16384, BF16)
    env.dwtl = aview(16384, 16384, BF16)
    env.sD = [aview(40960, 4096, F32), aview(45056, 4096, F32)]

    P = lambda n, s, d: nc.alloc_psum_tensor(n, s, d).ap()
    env.ph = [P("ph0", [128, 512], F32), P("ph1", [128, 512], F32)]
    env.po = [P(f"po{i}", [128, 512], F32) for i in range(4)]
    env.pl = [P("pl0", [128, 512], F32), P("pl1", [128, 512], F32)]

    env.t = Sem(nc.alloc_semaphore("s_t"))
    env.s = Sem(nc.alloc_semaphore("s_s"))
    env.v = Sem(nc.alloc_semaphore("s_v"))
    env.din = Sem(nc.alloc_semaphore("s_din"))
    env.wst = Sem(nc.alloc_semaphore("s_wst"))
    env.sto = Sem(nc.alloc_semaphore("s_sto"))
    env.outs = Sem(nc.alloc_semaphore("s_outs"))
    env.all_sems = [env.t, env.s, env.v, env.din, env.wst, env.sto, env.outs]

    env.dbg = {}
    if dbg:
        for nm, shape, dt in [("dYhi", [128, DC * T], BF16), ("dYlo", [128, DC * T], BF16),
                              ("dx1", [128, c.TCH * D], F32), ("dx2", [128, c.TCH * D], F32),
                              ("dqa", [128, c.FC], F32), ("dqb", [128, c.FC], F32),
                              ("dtabAr", [128, c.NSCAN], F32), ("dtabAi", [128, c.NSCAN], F32),
                              ("dS0", [128, T], F32), ("dH", [128, T], F32)]:
            env.dbg[nm] = nc.dram_tensor(nm, shape, dt, kind="ExternalOutput").ap()
    with nc.Block() as block:
        run_phase(block, env, phase_prologue)
        run_phase(block, env, phase_ln)
        if dbg:
            run_phase(block, env, mk_dump("dYhi", lambda env: env.Yhi))
            run_phase(block, env, mk_dump("dYlo", lambda env: env.Ylo))
            run_phase(block, env, mk_dump("dqa", lambda env: env.qa))
            run_phase(block, env, mk_dump("dqb", lambda env: env.qb))
            run_phase(block, env, mk_dump("dtabAr", lambda env: env.tabAr))
            run_phase(block, env, mk_dump("dtabAi", lambda env: env.tabAi))
        run_phase(block, env, lambda env, e: phase_ffn(env, e, 0))
        if dbg:
            run_phase(block, env, mk_dump("dx1", lambda env: env.x))
        run_phase(block, env, phase_ln)
        run_phase(block, env, phase_ssm)
        if dbg:
            run_phase(block, env, mk_dump("dx2", lambda env: env.x))
        run_phase(block, env, phase_ln)
        run_phase(block, env, lambda env, e: phase_ffn(env, e, 1))
        run_phase(block, env, phase_out)
    return nc



def prep_inputs(T, DF, u_b, lAr, lAi, Bw, Cw, Dw, W1, b1, ph, qwr, qwi, W2, b2):
    FC = DF // 128
    f32 = np.float32
    cont = np.ascontiguousarray
    W1p = np.asarray(W1, f32).reshape(DC, 128, FC, 128).transpose(2, 1, 0, 3) \
        .reshape(FC, 128, DC * 128)
    W2p = np.asarray(W2, f32).reshape(FC, 128, D)
    wpack = cont(np.concatenate([W1p, W2p], axis=2).reshape(FC * 128, 2048))
    colmaj = lambda v: cont(np.asarray(v, f32).reshape(FC, 128).T)
    return {
        "u": cont(np.asarray(u_b, f32)),
        "lAr": cont(np.asarray(lAr, f32).reshape(NS, 1)),
        "lAi": cont(np.asarray(lAi, f32).reshape(NS, 1)),
        "bwt": cont(np.asarray(Bw, f32).T),
        "cwt": cont(np.asarray(Cw, f32).T),
        "dwt": cont(np.asarray(Dw, f32).T),
        "wpack": wpack,
        "b1": colmaj(b1),
        "ph": colmaj(ph),
        "qwr": colmaj(qwr),
        "qwi": colmaj(qwi),
        "b2": cont(np.asarray(b2, f32).reshape(1, D)),
        "idf": np.eye(128, dtype=f32),
        "idb": np.eye(128, dtype=ml_dtypes.bfloat16),
    }


def kernel(L0_attn_u, L0_attn_log_A_real, L0_attn_log_A_imag, L0_attn_B_w,
           L0_attn_C_w, L0_attn_D_w, L0_ffn_W1, L0_ffn_b1, L0_ffn_quantum_phases,
           L0_ffn_quantum_weights_real, L0_ffn_quantum_weights_imag, L0_ffn_W2,
           L0_ffn_b2):
    T, DF = 2048, 4096
    u = np.asarray(L0_attn_u, np.float32)
    B = u.shape[0]
    nc = build(T, DF)
    in_maps = [
        prep_inputs(T, DF, u[b], L0_attn_log_A_real, L0_attn_log_A_imag,
                    L0_attn_B_w, L0_attn_C_w, L0_attn_D_w, L0_ffn_W1, L0_ffn_b1,
                    L0_ffn_quantum_phases, L0_ffn_quantum_weights_real,
                    L0_ffn_quantum_weights_imag, L0_ffn_W2, L0_ffn_b2)
        for b in range(B)
    ]
    res = run_bass_kernel_spmd(nc, in_maps, core_ids=list(range(B)))
    return np.stack([np.asarray(r["out"], np.float32) for r in res.results], axis=0)
